# version 38
# baseline (speedup 1.0000x reference)
import numpy as np
import ml_dtypes
import concourse.bass as bass
import concourse.mybir as mybir
from concourse.bass_utils import run_bass_kernel_spmd
from contextlib import ExitStack

F32 = mybir.dt.float32
BF16 = mybir.dt.bfloat16
I8 = mybir.dt.int8
U8 = mybir.dt.uint8
AF = mybir.ActivationFunctionType
ALU = mybir.AluOpType

N_CORES = 8
D = 2048
CH = 16
NC = 32
M = 64
J = 16
EPS = 1e-5

ENGINES = ("sync", "vector", "scalar", "tensor", "gpsimd")


class SemCtr:
    _n = 0

    def __init__(self, name=None):
        self.name = name or f"sc{SemCtr._n}"
        SemCtr._n += 1
        self.val = 0
        self.handle = None


class Prog:

    def __init__(self):
        self.ops = {e: [] for e in ENGINES}
        self.ectr = {e: SemCtr("ctr_" + e) for e in ENGINES}
        self.watermark = {e: {} for e in ENGINES}
        self.all_sems = list(self.ectr.values())

    def new_sem(self, name=None):
        s = SemCtr(name)
        self.all_sems.append(s)
        return s

    def op(self, eng, emit, deps=(), dma_sem=None, noinc=False):
        sc = dma_sem if dma_sem is not None else self.ectr[eng]
        step = 16 if dma_sem is not None else 1
        if dma_sem is None and self.ectr[eng].val > 0:
            deps = list(deps) + [(self.ectr[eng], self.ectr[eng].val)]
        if noinc:
            best = {}
            for d in deps:
                if d is None:
                    continue
                s2, v2 = d
                if v2 > best.get(id(s2), (None, 0))[1]:
                    best[id(s2)] = (s2, v2)
            needed = []
            wm = self.watermark[eng]
            for s2, v2 in best.values():
                if v2 > wm.get(id(s2), 0):
                    needed.append((s2, v2))
                    wm[id(s2)] = v2
            self.ops[eng].append((emit, needed, None, 0))
            return None
        sc.val += step
        tok = (sc, sc.val)
        best = {}
        for d in deps:
            if d is None:
                continue
            s2, v2 = d
            if v2 > best.get(id(s2), (None, 0))[1]:
                best[id(s2)] = (s2, v2)
        needed = []
        wm = self.watermark[eng]
        for s2, v2 in best.values():
            if v2 > wm.get(id(s2), 0):
                needed.append((s2, v2))
                wm[id(s2)] = v2
        self.ops[eng].append((emit, needed, sc, step))
        return tok

    def emit_engine(self, eng_obj, eng_name, tail=None):
        for emit, needed, sc, step in self.ops[eng_name]:
            for (s2, v2) in needed[:-1]:
                eng_obj.wait_ge(s2.handle, v2)
            inst = emit(eng_obj)
            if needed:
                inst._wait_ge(needed[-1][0].handle, needed[-1][1])
            if sc is not None:
                inst.then_inc(sc.handle, step)
        if tail is not None:
            tail(eng_obj)


class Slot:
    def __init__(self):
        self.wr = None
        self.rds = []

    def wdeps(self):
        return ([self.wr] if self.wr else []) + list(self.rds)

    def wrote(self, tok):
        self.wr = tok
        self.rds = []

    def rdeps(self):
        return [self.wr] if self.wr else []

    def read(self, tok):
        self.rds.append(tok)


def host_pack(inputs):
    x = np.ascontiguousarray(np.asarray(inputs["L0_attn_x"], dtype=np.float32).reshape(-1, D))
    Wqkv = np.asarray(inputs["L0_attn_Wqkv"], dtype=np.float32)
    bqkv = np.asarray(inputs["L0_attn_bqkv"], dtype=np.float32)
    Wout = np.asarray(inputs["L0_attn_Wout"], dtype=np.float32)
    bout = np.asarray(inputs["L0_attn_bout"], dtype=np.float32)
    wg = np.asarray(inputs["L0_ffn_w_gate"])
    wu = np.asarray(inputs["L0_ffn_w_up"])
    wd = np.asarray(inputs["L0_ffn_w_down"])
    sg = np.asarray(inputs["L0_ffn_s_gate"], dtype=np.float32)
    su = np.asarray(inputs["L0_ffn_s_up"], dtype=np.float32)
    sd = np.asarray(inputs["L0_ffn_s_down"], dtype=np.float32)

    wqkv_bd = np.zeros((CH, 3, 128, 128), dtype=np.float32)
    for ch in range(CH):
        for s in range(3):
            wqkv_bd[ch, s, 0:64, 0:64] = Wqkv[2 * ch, :, s * 64:(s + 1) * 64]
            wqkv_bd[ch, s, 64:128, 64:128] = Wqkv[2 * ch + 1, :, s * 64:(s + 1) * 64]
    wqkv_bd = np.ascontiguousarray(wqkv_bd.reshape(CH * 3 * 128, 128))

    p = np.arange(128)
    chn = np.arange(CH)
    bq_col = np.ascontiguousarray(bqkv[(2 * chn[None, :] + p[:, None] // 64), (p[:, None] % 64)])
    bk_col = np.ascontiguousarray(bqkv[(2 * chn[None, :] + p[:, None] // 64), 64 + (p[:, None] % 64)])
    bv_col = np.ascontiguousarray(bqkv[(2 * chn[None, :] + p[:, None] // 64), 128 + (p[:, None] % 64)])

    wout_bd = np.zeros((128, 128), dtype=np.float32)
    wout_bd[0:64, 0:64] = Wout
    wout_bd[64:128, 64:128] = Wout
    bout_col = np.ascontiguousarray(bout[p % 64].reshape(128, 1))

    wg4 = wg.reshape(CH, 128, M, 128).transpose(2, 1, 0, 3)
    wu4 = wu.reshape(CH, 128, M, 128).transpose(2, 1, 0, 3)
    wgu_pk = np.ascontiguousarray(np.stack([wg4, wu4], axis=2).reshape(M, 128, 2 * CH * 128))
    wd_pk = np.ascontiguousarray(wd.reshape(M, 128, J, 128).transpose(2, 1, 0, 3).reshape(J, 128, M * 128))

    sg_col = np.ascontiguousarray(sg.reshape(M, 128).T)
    su_col = np.ascontiguousarray(su.reshape(M, 128).T)
    sd_col = np.ascontiguousarray(sd.reshape(J, 128).T)

    return x, dict(wqkv_bd=wqkv_bd, bq_col=bq_col, bk_col=bk_col, bv_col=bv_col,
                   wout_bd=wout_bd, bout_col=bout_col, wgu_pk=wgu_pk, wd_pk=wd_pk,
                   sg_col=sg_col, su_col=su_col, sd_col=sd_col)


def build_program(NTOK):
    TT = 512
    TA = 128
    NTT = NTOK // TT
    NTA = NTOK // TA

    nc = bass.Bass("TRN2", target_bir_lowering=False)
    P = Prog()

    x_fm = nc.dram_tensor("x_fm", [D, NTOK], F32, kind="ExternalInput").ap()
    wqkv_in = nc.dram_tensor("wqkv_bd", [CH * 3 * 128, 128], F32, kind="ExternalInput").ap()
    bq_in = nc.dram_tensor("bq_col", [128, CH], F32, kind="ExternalInput").ap()
    bk_in = nc.dram_tensor("bk_col", [128, CH], F32, kind="ExternalInput").ap()
    bv_in = nc.dram_tensor("bv_col", [128, CH], F32, kind="ExternalInput").ap()
    wout_in = nc.dram_tensor("wout_bd", [128, 128], F32, kind="ExternalInput").ap()
    bout_in = nc.dram_tensor("bout_col", [128, 1], F32, kind="ExternalInput").ap()
    wgu_in = nc.dram_tensor("wgu_pk", [M, 128, 2 * CH * 128], I8, kind="ExternalInput").ap()
    wd_in = nc.dram_tensor("wd_pk", [J, 128, M * 128], I8, kind="ExternalInput").ap()
    sg_in = nc.dram_tensor("sg_col", [128, M], F32, kind="ExternalInput").ap()
    su_in = nc.dram_tensor("su_col", [128, M], F32, kind="ExternalInput").ap()
    sd_in = nc.dram_tensor("sd_col", [128, J], F32, kind="ExternalInput").ap()
    y_fm = nc.dram_tensor("y_fm", [D, NTOK], F32, kind="ExternalOutput").ap()

    fuse_d = nc.dram_tensor("fuse_d", [M, 128, NTOK], BF16, kind="Internal").ap()
    s_d = nc.dram_tensor("s_d", [J, 128, NTOK], BF16, kind="Internal").ap()

    with ExitStack() as ctx:
        def sb(name, shape, dt):
            return ctx.enter_context(nc.sbuf_tensor(name, shape, dt))

        ones_col = sb("ones_col", [128, 1], F32)
        eps_col = sb("eps_col", [128, 1], F32)
        ones_row = sb("ones_row", [1, 128], F32)
        wqkv_b = sb("wqkv_b", [128, CH * 3 * 128], BF16)
        wout_b = sb("wout_b", [128, 128], BF16)
        bq8 = sb("bq8", [128, CH], F32)
        bk_c = sb("bk_c", [128, CH], F32)
        bv_c = sb("bv_c", [128, CH], F32)
        bout_c = sb("bout_c", [128, 1], F32)
        sg_c = sb("sg_c", [128, M], F32)
        su_c = sb("su_c", [128, M], F32)
        sd_c = sb("sd_c", [128, J], F32)

        xn = sb("xn", [128, max(CH * NTOK, M * TT)], BF16)

        af32 = sb("af32", [128, 11264], F32)
        xln_v = af32[:, 0:8192]
        lnrow = af32[:, 8192:10240]
        sq_v = [af32[:, 10240 + i * 512:10240 + (i + 1) * 512] for i in range(2)]
        xfin_v = [af32[:, 8192 + i * 512:8192 + (i + 1) * 512] for i in range(2)]

        q_sb2 = sb("q_sb2", [128, 2 * CH * TA], BF16)
        k_sb2 = sb("k_sb2", [128, 2 * CH * TA], BF16)
        v_sb2 = sb("v_sb2", [128, 2 * CH * TA], BF16)
        qd_s = sb("qd_s", [128, TA], BF16)
        E_s = sb("E_s", [128, 4 * CH * TA], BF16)
        zt1 = sb("zt1", [128, 8 * TA], BF16)
        kmx = zt1[:, 3 * TA:4 * TA]
        Zall = sb("Zall", [128, 4 * TA], BF16)
        u_dup = zt1[:, 4 * TA:5 * TA]
        ctx_s = sb("ctx_s", [128, CH * TA], BF16)
        s_stage = sb("s_stage", [128, 4 * TA], BF16)
        hfold = zt1[:, 2 * TA:3 * TA]

        abf = sb("abf", [128, 11264], BF16)
        wgu3 = sb("wgu3", [128, 4096], BF16)
        wgubf_v = [abf[:, 0:4096], abf[:, 4096:8192], wgu3[:, :]]
        g_v = [abf[:, 8192 + i * 512:8192 + (i + 1) * 512] for i in range(2)]
        uu_v = [abf[:, 9216 + i * 512:9216 + (i + 1) * 512] for i in range(2)]
        fuse_st_v = [abf[:, 10240 + i * 512:10240 + (i + 1) * 512] for i in range(2)]
        wdq_v = [abf[:, i * 2048:(i + 1) * 2048] for i in range(3)]
        sfin_v = [abf[:, 6144 + i * 512:6144 + (i + 1) * 512] for i in range(2)]
        gfin_v = [abf[:, 7168 + i * 512:7168 + (i + 1) * 512] for i in range(2)]
        sgm_v = [abf[:, 8192 + i * 512:8192 + (i + 1) * 512] for i in range(2)]

        fuse_tt_v = xn[:, 0:M * TT]

        pb = [ctx.enter_context(nc.psum_tensor(f"pb{i}", [128, 512], F32)) for i in range(8)]

        sl_xln = Slot()
        sl_lnrow = Slot()
        sl_sq = [Slot(), Slot()]
        sl_q2 = [Slot(), Slot()]
        sl_k2 = [Slot(), Slot()]
        sl_v2 = [Slot(), Slot()]
        sl_qd = [Slot(), Slot()]
        sl_kmx = Slot()
        sl_E = [Slot(), Slot(), Slot(), Slot()]
        sl_zt1, sl_Z, sl_ud = Slot(), Slot(), Slot()
        sl_ctx = Slot()
        sl_hf = Slot()
        _ss = Slot()
        sl_sst = [_ss, _ss]
        sl_wgubf = [Slot(), Slot(), Slot()]
        sl_g = [Slot(), Slot()]
        sl_uu = [Slot(), Slot()]
        sl_fst = [Slot(), Slot()]
        sl_pb = [Slot() for _ in range(8)]
        sl_wdq = [Slot(), Slot(), Slot()]
        sl_sfin = [Slot(), Slot()]
        sl_xfin = [Slot(), Slot()]
        sl_gfin = [Slot(), Slot()]
        sl_sgm = [Slot(), Slot()]
        sl_fuse_tt = Slot()
        sl_misc = Slot()

        dma_init = P.new_sem("d_init")
        dma_init2 = [P.new_sem(f"d_init2_{i}") for i in range(7)]
        dma_x = P.new_sem("d_x")
        dma_wgu = [P.new_sem(f"d_wgu{i}") for i in range(3)]
        dma_fst = [P.new_sem(f"d_fst{i}") for i in range(4)]
        dma_sst = [P.new_sem(f"d_sst{i}") for i in range(2)]
        dma_ftt = [P.new_sem(f"d_ftt{i}") for i in range(4)]
        dma_wdq = [P.new_sem(f"d_wdq{i}") for i in range(3)]
        dma_sfin = [P.new_sem(f"d_sfin{i}") for i in range(2)]
        dma_xfin = [P.new_sem(f"d_xfin{i}") for i in range(2)]
        dma_y = [P.new_sem(f"d_y{i}") for i in range(4)]

        cast_toks = []
        for rnd in range(3):
            t = P.op("sync", lambda e, r=rnd: e.dma_start(
                out=lnrow[:, :].rearrange("p (c n) -> p c n", c=16),
                in_=wqkv_in.rearrange("(c p) n -> p c n", p=128)[:, r * 16:(r + 1) * 16, :]),
                deps=sl_lnrow.wdeps(), dma_sem=dma_init)
            sl_lnrow.wrote(t)
            for ci in range(16):
                blk = rnd * 16 + ci
                sc = 0.125 if blk % 3 == 0 else 1.0
                tc = P.op("scalar", lambda e, b=blk, ci=ci, sc=sc: e.activation(
                    wqkv_b[:, b * 128:(b + 1) * 128], lnrow[:, ci * 128:(ci + 1) * 128],
                    AF.Copy, scale=sc), deps=[t])
                sl_lnrow.read(tc)
                cast_toks.append(tc)
        ld = []
        for li, (dst, src) in enumerate([(bq8, bq_in), (bk_c, bk_in), (bv_c, bv_in),
                           (bout_c, bout_in), (sg_c, sg_in), (su_c, su_in), (sd_c, sd_in)]):
            ld.append(P.op("sync", lambda e, d=dst, s=src: e.dma_start(out=d[:], in_=s),
                           dma_sem=dma_init2[li]))
        t_oc = P.op("vector", lambda e: e.memset(ones_col[:], 1.0))
        t_eps = P.op("vector", lambda e: e.memset(eps_col[:], float(EPS)))
        t_or = P.op("vector", lambda e: e.memset(ones_row[:], 1.0))
        tb = P.op("vector", lambda e: e.tensor_scalar_mul(bq8[:], bq8[:], 0.125), deps=ld)
        tb2 = P.op("vector", lambda e: e.tensor_scalar_mul(sg_c[:], sg_c[:], 0.5), deps=ld)
        tb3 = P.op("vector", lambda e: e.tensor_tensor(su_c[:], su_c[:], sg_c[:], ALU.mult), deps=[tb2])
        tb4 = P.op("vector", lambda e: e.tensor_scalar_mul(sd_c[:], sd_c[:], 0.5), deps=ld)
        sgh_c, suf_c, sdh_c = sg_c, su_c, sd_c
        t = P.op("sync", lambda e: e.dma_start(out=lnrow[:, 0:128], in_=wout_in),
                 deps=sl_lnrow.wdeps(), dma_sem=dma_init)
        sl_lnrow.wrote(t)
        tw = P.op("scalar", lambda e: e.activation(wout_b[:], lnrow[:, 0:128], AF.Copy), deps=[t])
        sl_lnrow.read(tw)
        sl_misc.wrote(tw)
        init_small = [(sm, sm.val) for sm in dma_init2] + [tb, tb3, tb4]

        xn_ready = [None] * NTT

        def ln_unit(tt):
            src = x_fm.rearrange("(c p) n -> p c n", p=128)[:, :, tt * TT:(tt + 1) * TT]
            dl = P.op("sync", lambda e: e.dma_start(
                out=xln_v.rearrange("p (c n) -> p c n", c=CH), in_=src),
                deps=sl_xln.wdeps(), dma_sem=dma_x)
            sl_xln.wrote(dl)
            mean_ps, sq_ps, rrep_ps, mrep_ps = pb[4], pb[5], pb[6], pb[7]
            tmean = tsq = None
            for chi in range(CH):
                xc = xln_v[:, chi * TT:(chi + 1) * TT]
                tmean = P.op("tensor", lambda e, xc=xc, f=(chi == 0), l=(chi == CH - 1): e.matmul(
                    mean_ps[0:1, 0:TT], ones_col[:], xc, start=f, stop=l),
                    deps=([dl, t_oc] + sl_pb[4].wdeps()) if chi == 0 else [],
                    noinc=(chi < CH - 1))
                ssl = chi % 2
                tq = P.op("scalar", lambda e, xc=xc, s=ssl: e.activation(sq_v[s][:, 0:TT], xc, AF.Square),
                          deps=[dl] + sl_sq[ssl].wdeps())
                sl_sq[ssl].wrote(tq)
                tsq = P.op("tensor", lambda e, f=(chi == 0), l=(chi == CH - 1), s=ssl: e.matmul(
                    sq_ps[0:1, 0:TT], ones_col[:], sq_v[s][:, 0:TT], start=f, stop=l),
                    deps=[tq] + (sl_pb[5].wdeps() if chi == 0 else []))
                sl_sq[ssl].read(tsq)
            sl_pb[4].wrote(tmean)
            sl_pb[5].wrote(tsq)
            r_mean = lnrow[0:1, 0:TT]
            r_m2 = lnrow[0:1, TT:2 * TT]
            r_var = lnrow[0:1, 2 * TT:3 * TT]
            r_r = lnrow[0:1, 3 * TT:4 * TT]
            t1 = P.op("vector", lambda e: e.tensor_scalar_mul(r_mean, mean_ps[0:1, 0:TT], 1.0 / D),
                      deps=[tmean] + sl_lnrow.wdeps())
            sl_pb[4].read(t1)
            t2 = P.op("vector", lambda e: e.tensor_tensor(r_m2, r_mean, r_mean, ALU.mult), deps=[t1])
            t3 = P.op("vector", lambda e: e.tensor_scalar_mul(r_var, sq_ps[0:1, 0:TT], 1.0 / D),
                      deps=[tsq])
            sl_pb[5].read(t3)
            t4 = P.op("vector", lambda e: e.tensor_tensor(r_var, r_var, r_m2, ALU.subtract), deps=[t2, t3])
            t5 = P.op("scalar", lambda e: e.activation(r_m2, r_var, AF.Ln, bias=eps_col[0:1, :]), deps=[t4, t_eps])
            t6 = P.op("scalar", lambda e: e.activation(r_r, r_m2, AF.Exp, scale=-0.5), deps=[t5])
            sl_lnrow.wrote(t6)
            t7 = P.op("tensor", lambda e: e.matmul(rrep_ps[:, 0:TT], ones_row[:], r_r, start=True, stop=True),
                      deps=[t6, t_or] + sl_pb[6].wdeps())
            sl_pb[6].wrote(t7)
            sl_lnrow.read(t7)
            t8 = P.op("tensor", lambda e: e.matmul(mrep_ps[:, 0:TT], ones_row[:], r_mean, start=True, stop=True),
                      deps=[t6] + sl_pb[7].wdeps())
            sl_pb[7].wrote(t8)
            sl_lnrow.read(t8)
            last = None
            for chi in range(CH):
                xc = xln_v[:, chi * TT:(chi + 1) * TT]
                xno = xn[:, chi * NTOK + tt * TT: chi * NTOK + (tt + 1) * TT]
                ta = P.op("vector", lambda e, xc=xc, o=xno: e.tensor_tensor(o, xc, mrep_ps[:, 0:TT], ALU.subtract),
                          deps=[t8, dl])
                tbm = P.op("vector", lambda e, o=xno: e.tensor_tensor(o, o, rrep_ps[:, 0:TT], ALU.mult),
                           deps=[t7])
                last = tbm
                sl_pb[6].read(tbm)
                sl_pb[7].read(ta)
                sl_xln.read(tbm)
            xn_ready[tt] = last
            return last

        s_store_toks = []

        def att_pre_group(ti, gi):
            t0 = ti * TA
            po = (ti % 2) * CH * TA
            dep_xn = [xn_ready[t0 // TT]]
            outs = {0: q_sb2, 1: k_sb2, 2: v_sb2}
            slo = {0: sl_q2[ti % 2], 1: sl_k2[ti % 2], 2: sl_v2[ti % 2]}
            items = [(s, chi) for s in range(3) for chi in range(CH)]
            four = items[gi * 4:(gi + 1) * 4]
            bank = gi % 2
            s = four[0][0]
            ch0 = four[0][1]
            mmt = None
            for li, (s_, chi) in enumerate(four):
                woff = (chi * 3 + s_) * 128
                rv = xn[:, chi * NTOK + t0: chi * NTOK + t0 + TA]
                dep = (dep_xn + sl_pb[bank].wdeps() + sl_misc.rdeps()) if li == 0 else []
                mmt = P.op("tensor", lambda e, w=woff, rv=rv, li=li, b=bank: e.matmul(
                    pb[b][:, li * TA:(li + 1) * TA], wqkv_b[:, w:w + 128], rv,
                    start=True, stop=True), deps=dep, noinc=(li < 3))
            sl_pb[bank].wrote(mmt)
            ev = P.op("scalar", lambda e, b=bank, d=outs[s], c0=ch0: e.activation(
                d[:, po + c0 * TA:po + (c0 + 4) * TA], pb[b][:, 0:4 * TA], AF.Copy),
                deps=[mmt] + slo[s].wdeps())
            sl_pb[bank].read(ev)
            slo[s].rds.append(ev)
            if gi == 11:
                for s2 in range(3):
                    slo[s2].wrote(slo[s2].rds[-1])

        def att_pre(ti):
            t0 = ti * TA
            po = (ti % 2) * CH * TA
            k_sb = k_sb2[:, po:po + CH * TA]
            sl_k = sl_k2[ti % 2]
            tkp = None
            for chi in range(CH):
                kv = k_sb[:, chi * TA:(chi + 1) * TA]
                tkp = P.op("vector", lambda e, kv=kv, c=chi: e.tensor_scalar(
                    kv, kv, bk_c[:, c:c + 1], None, ALU.add),
                    deps=(sl_k.wdeps() + sl_k.rdeps() + init_small) if chi == 0 else [])
            sl_k.wrote(tkp)
            return k_sb[:, :].rearrange("p (j n) -> p j n", j=CH)

        pair_st = {}

        qd2 = zt1[:, 6 * TA:8 * TA]

        def att_pair_fm(ti, i, kall3):
            po = (ti % 2) * CH * TA
            q_sb = q_sb2[:, po:po + CH * TA]
            k_sb = k_sb2[:, po:po + CH * TA]
            sl_q = sl_q2[ti % 2]
            sl_k = sl_k2[ti % 2]
            sl0 = (i % 2) * 2
            tq = None
            for h in (0, 1):
                qsrc = q_sb[64 * h:64 * (h + 1), i * TA:(i + 1) * TA]
                bqv = bq8[64 * h:64 * h + 64, i:i + 1]
                tq1 = P.op("vector", lambda e, qs=qsrc, b=bqv, h=h: e.tensor_scalar(
                    qd2[0:64, h * TA:(h + 1) * TA], qs, b, None, ALU.add),
                    deps=sl_q.rdeps() + sl_qd[h].wdeps() + init_small + sl_zt1.wdeps())
                tq = P.op("vector", lambda e, qs=qsrc, b=bqv, h=h: e.tensor_scalar(
                    qd2[64:128, h * TA:(h + 1) * TA], qs, b, None, ALU.add), deps=[])
                sl_q.read(tq)
                sl_qd[h].wrote(tq)
                sl_zt1.read(tq)
            ev2 = E_s[:, sl0 * CH * TA:(sl0 + 2) * CH * TA]
            ev4 = ev2.rearrange("p (h j n) -> p h j n", h=2, j=CH)
            qb4 = qd2.rearrange("p (h n) -> p h n", h=2).unsqueeze(2).broadcast_to([128, 2, CH, TA])
            kb4 = kall3.unsqueeze(1).broadcast_to([128, 2, CH, TA]) if False else None
            k4 = k_sb[:, :].rearrange("p (j n) -> p j n", j=CH).unsqueeze(1).broadcast_to([128, 2, CH, TA])
            tE = P.op("vector", lambda e: e.tensor_tensor(ev4, qb4, k4, ALU.mult),
                      deps=sl_k.rdeps() + sl_E[sl0].wdeps() + sl_E[sl0 + 1].wdeps() + [tq])
            sl_k.read(tE)
            sl_qd[0].read(tE)
            sl_qd[1].read(tE)
            sl_zt1.read(tE)
            tX = P.op("scalar", lambda e, ev=ev2: e.activation(ev, ev, AF.Exp), deps=[tE])
            sl_E[sl0].wrote(tX)
            sl_E[sl0 + 1].wrote(tX)
            pair_st[(ti, i)] = [tX, tX]

        def att_pair_tree(ti, i):
            exp_toks = pair_st.pop((ti, i))
            tz_pair = []
            for h in (0, 1):
                sl = (i % 2) * 2 + h
                ev = E_s[:, sl * CH * TA:(sl + 1) * CH * TA]
                tz = P.op("vector", lambda e, ev=ev: e.tensor_tensor(
                    zt1[:, 0:8 * TA], ev[:, 0:8 * TA], ev[:, 8 * TA:16 * TA], ALU.add),
                    deps=[exp_toks[h]] + sl_zt1.wdeps())
                tz = P.op("vector", lambda e: e.tensor_tensor(
                    zt1[:, 0:4 * TA], zt1[:, 0:4 * TA], zt1[:, 4 * TA:8 * TA], ALU.add), deps=[tz])
                tz = P.op("vector", lambda e: e.tensor_tensor(
                    zt1[:, 0:2 * TA], zt1[:, 0:2 * TA], zt1[:, 2 * TA:4 * TA], ALU.add), deps=[tz])
                tz = P.op("vector", lambda e: e.tensor_tensor(
                    zt1[:, 0:TA], zt1[:, 0:TA], zt1[:, TA:2 * TA], ALU.add), deps=[tz])
                zdst = Zall[64 * h:64 * h + 64, (i % 4) * TA:(i % 4 + 1) * TA]
                tz = P.op("vector", lambda e: e.tensor_copy(hfold[0:64, :], zt1[64:128, 0:TA]),
                          deps=[tz] + sl_hf.wdeps())
                sl_hf.wrote(tz)
                tz = P.op("vector", lambda e, zd=zdst: e.tensor_tensor(
                    zd, zt1[0:64, 0:TA], hfold[0:64, :], ALU.add),
                    deps=[tz] + sl_Z.wdeps())
                sl_hf.read(tz)
                sl_zt1.wrote(tz)
                sl_Z.rds.append(tz)
                tz_pair.append(tz)
            zcol = Zall[:, (i % 4) * TA:(i % 4 + 1) * TA]
            def _recip(e, zc=zcol):
                with nc.allow_low_precision(reason="1/Z on bf16 softmax denominator"):
                    return e.reciprocal(zc, zc)
            tr = P.op("vector", _recip, deps=tz_pair)
            sl_Z.rds.append(tr)
            pair_st[(ti, i, "r")] = tr

        u_dup2 = zt1[:, 4 * TA:6 * TA]

        def att_pair_back(ti, i):
            po = (ti % 2) * CH * TA
            v_sb = v_sb2[:, po:po + CH * TA]
            sl_v = sl_v2[ti % 2]
            tr = pair_st.pop((ti, i, "r"))
            sl0 = (i % 2) * 2
            tu = None
            for h in (0, 1):
                vsrc = v_sb[64 * h:64 * (h + 1), i * TA:(i + 1) * TA]
                rzv = Zall[64 * h:64 * h + 64, (i % 4) * TA:(i % 4 + 1) * TA]
                bvv = bv_c[64 * h:64 * h + 64, i:i + 1]
                tu1 = P.op("vector", lambda e, vs=vsrc, rz=rzv, b=bvv, h=h: e.scalar_tensor_tensor(
                    u_dup2[0:64, h * TA:(h + 1) * TA], vs, b, rz, ALU.add, ALU.mult),
                    deps=sl_v.rdeps() + [tr] + sl_ud.wdeps() + init_small)
                tu = P.op("vector", lambda e, vs=vsrc, rz=rzv, b=bvv, h=h: e.scalar_tensor_tensor(
                    u_dup2[64:128, h * TA:(h + 1) * TA], vs, b, rz, ALU.add, ALU.mult), deps=[])
                sl_v.read(tu)
                sl_Z.rds.append(tu)
            sl_ud.wrote(tu)
            ev2 = E_s[:, sl0 * CH * TA:(sl0 + 2) * CH * TA]
            ev4 = ev2.rearrange("p (h j n) -> p h j n", h=2, j=CH)
            ub4 = u_dup2.rearrange("p (h n) -> p h n", h=2).unsqueeze(2).broadcast_to([128, 2, CH, TA])
            tt_ = P.op("vector", lambda e: e.tensor_tensor(ev4, ub4, ev4, ALU.mult),
                       deps=[tu] + sl_E[sl0].rdeps() + sl_E[sl0 + 1].rdeps())
            sl_ud.read(tt_)
            sl_zt1.read(tt_)
            sl_E[sl0].wrote(tt_)
            sl_E[sl0 + 1].wrote(tt_)
            for h in (0, 1):
                ev = E_s[:, (sl0 + h) * CH * TA:(sl0 + h + 1) * CH * TA]
                if i == 0 and h == 0:
                    tc = P.op("vector", lambda e, ev=ev: e.tensor_copy(ctx_s[:], ev),
                              deps=[tt_] + sl_ctx.wdeps())
                else:
                    tc = P.op("vector", lambda e, ev=ev: e.tensor_tensor(ctx_s[:], ctx_s[:], ev, ALU.add),
                              deps=[tt_])
                sl_E[sl0 + h].read(tc)
                sl_ctx.wrote(tc)

        def att_proj(ti):
            t0 = ti * TA
            sl_Z.wrote(sl_Z.rds[-1])
            for g4 in range(4):
                bank = g4 % 2
                mmt = None
                for li in range(4):
                    jj = g4 * 4 + li
                    rv = ctx_s[:, jj * TA:(jj + 1) * TA]
                    dep = ([sl_ctx.wr] + sl_pb[bank].wdeps() + sl_misc.rdeps()) if li == 0 else []
                    mmt = P.op("tensor", lambda e, rv=rv, li=li, b=bank: e.matmul(
                        pb[b][:, li * TA:(li + 1) * TA], wout_b[:], rv, start=True, stop=True),
                        deps=dep, noinc=(li < 3))
                sl_pb[bank].wrote(mmt)
                sl_ctx.read(mmt)
                ssl = g4 % 2
                ev = P.op("scalar", lambda e, b=bank, s=ssl: e.activation(
                    s_stage[:, 0:4 * TA], pb[b][:, 0:4 * TA], AF.Identity,
                    bias=bout_c[:, :]),
                    deps=[mmt] + sl_sst[ssl].wdeps() + init_small)
                sl_pb[bank].read(ev)
                sl_sst[ssl].wrote(ev)
                sdst = s_d[g4 * 4:(g4 + 1) * 4, :, t0:t0 + TA].rearrange("j p n -> p j n")
                ssrc = s_stage[:, 0:4 * TA].rearrange("p (j n) -> p j n", j=4)
                st = P.op("scalar", lambda e, d=sdst, s=ssrc: e.dma_start(out=d, in_=s),
                          deps=[ev], dma_sem=dma_sst[ssl])
                sl_sst[ssl].read(st)
                s_store_toks.append(st)

        def ffnA_load(m, extra_dep):
            slot = m % 3
            dl = P.op("gpsimd", lambda e, m=m, s=slot: e.dma_start(
                out=wgubf_v[s], in_=wgu_in[m]),
                deps=sl_wgubf[slot].wdeps() + extra_dep, dma_sem=dma_wgu[slot])
            sl_wgubf[slot].wrote(dl)
            return dl

        def ffnA_tt(m, tt, tc):
            slot = m % 3
            if True:
                u_ = m * NTT + tt
                gbank = (4, 5, 2)[u_ % 3]
                ubank = (6, 7, 3)[u_ % 3]
                mmg = mmu = None
                for chi in range(CH):
                    rv = xn[:, chi * NTOK + tt * TT: chi * NTOK + (tt + 1) * TT]
                    w_g = wgubf_v[slot][:, chi * 128:(chi + 1) * 128]
                    w_u = wgubf_v[slot][:, (CH + chi) * 128:(CH + chi + 1) * 128]
                    mmg = P.op("tensor", lambda e, w=w_g, rv=rv, f=(chi == 0), l=(chi == CH - 1), b=gbank: e.matmul(
                        pb[b][:, 0:TT], w, rv, start=f, stop=l),
                        deps=([tc, xn_ready[tt]] + sl_pb[gbank].wdeps()) if chi == 0 else [],
                        noinc=(chi < CH - 1))
                    mmu = P.op("tensor", lambda e, w=w_u, rv=rv, f=(chi == 0), l=(chi == CH - 1), b=ubank: e.matmul(
                        pb[b][:, 0:TT], w, rv, start=f, stop=l),
                        deps=sl_pb[ubank].wdeps() if chi == 0 else [],
                        noinc=(chi < CH - 1))
                sl_pb[gbank].wrote(mmg)
                sl_pb[ubank].wrote(mmu)
                sl_wgubf[slot].read(mmu)
                gsl = tt % 2
                fsl = tt % 2
                tg = P.op("scalar", lambda e, b=gbank, m=m, g=gsl: e.activation(
                    g_v[g], pb[b][:, 0:TT], AF.Tanh, scale=sgh_c[:, m:m + 1]),
                    deps=[mmg] + sl_g[gsl].wdeps() + init_small)
                sl_g[gsl].wrote(tg)
                ty = P.op("scalar", lambda e, b=gbank, f=fsl: e.activation(
                    fuse_st_v[f], pb[b][:, 0:TT], AF.Identity),
                    deps=[mmg] + sl_fst[fsl].wdeps())
                sl_pb[gbank].read(ty)
                sl_fst[fsl].wrote(ty)
                tu = P.op("scalar", lambda e, b=ubank, m=m, g=gsl: e.activation(
                    uu_v[g], pb[b][:, 0:TT], AF.Identity, scale=suf_c[:, m:m + 1]),
                    deps=[mmu] + sl_uu[gsl].wdeps() + init_small)
                sl_pb[ubank].read(tu)
                sl_uu[gsl].wrote(tu)
                def fuse_math(m=m, tt=tt, gsl=gsl, fsl=fsl, tg=tg, ty=ty, tu=tu):
                    tgr = P.op("vector", lambda e, g=gsl, f=fsl: e.scalar_tensor_tensor(
                        fuse_st_v[f], g_v[g], 1.0, fuse_st_v[f], ALU.add, ALU.mult),
                        deps=[tg, ty])
                    sl_g[gsl].read(tgr)
                    sl_fst[fsl].wrote(tgr)
                    tfu = P.op("gpsimd", lambda e, g=gsl, f=fsl: e.tensor_tensor(
                        fuse_st_v[f], fuse_st_v[f], uu_v[g], ALU.mult),
                        deps=[tgr, tu])
                    sl_uu[gsl].read(tfu)
                    sl_fst[fsl].wrote(tfu)
                    st = P.op("sync", lambda e, f=fsl: e.dma_start(
                        out=fuse_d[m, :, tt * TT:(tt + 1) * TT], in_=fuse_st_v[f]),
                        deps=[tfu], dma_sem=dma_fst[(m % 2) * 2 + (tt % 2)])
                    sl_fst[fsl].read(st)
                return fuse_math

        y_final = []

        def ffnB_unit(tt, first_deps):
            fl = []
            for q in range(4):
                src = fuse_d[q * 16:(q + 1) * 16, :, tt * TT:(tt + 1) * TT].rearrange("m p n -> p m n")
                dst = fuse_tt_v[:, q * 16 * TT:(q + 1) * 16 * TT].rearrange("p (m n) -> p m n", m=16)
                t = P.op("sync", lambda e, s=src, d=dst: e.dma_start(out=d, in_=s),
                         deps=sl_fuse_tt.wdeps() + first_deps if q == 0 else [],
                         dma_sem=dma_ftt[q])
                fl.append(t)
            sl_fuse_tt.wrote(fl[-1])
            sl_fuse_tt.rds = fl[:-1]
            qseq = [(j, q) for j in range(J) for q in range(4)]
            pend = {}

            def wd_load(idx):
                j, q = qseq[idx]
                wsl = idx % 3
                tl = P.op("gpsimd", lambda e, j=j, q=q, w=wsl: e.dma_start(
                    out=wdq_v[w], in_=wd_in[j][:, q * 2048:(q + 1) * 2048]),
                    deps=sl_wdq[wsl].wdeps() + (first_deps if tt == 0 and idx < 3 else []),
                    dma_sem=dma_wdq[wsl])
                sl_wdq[wsl].wrote(tl)
                pend[idx] = (wsl, tl)

            for idx in range(3):
                wd_load(idx)
            for j in range(J):
                bank = 4 + j % 4
                mmt = None
                for q in range(4):
                    idx = j * 4 + q
                    wsl, tl = pend.pop(idx)
                    for mi in range(16):
                        m = q * 16 + mi
                        w = wdq_v[wsl][:, mi * 128:(mi + 1) * 128]
                        rv = fuse_tt_v[:, m * TT:(m + 1) * TT]
                        dep = []
                        if m == 0:
                            dep = [tl, fl[q]] + sl_pb[bank].wdeps()
                        elif mi == 0:
                            dep = [tl, fl[q]]
                        mmt = P.op("tensor", lambda e, w=w, rv=rv, f=(m == 0), l=(m == M - 1), b=bank: e.matmul(
                            pb[b][:, 0:TT], w, rv, start=f, stop=l), deps=dep,
                            noinc=(mi < 15))
                    sl_wdq[wsl].read(mmt)
                    if idx + 3 < len(qseq):
                        wd_load(idx + 3)
                sl_pb[bank].wrote(mmt)
                sl_fuse_tt.read(mmt)
                gsl = j % 2
                ssl_t = j % 2
                tt1 = P.op("scalar", lambda e, b=bank, j=j, g=ssl_t: e.activation(
                    sgm_v[g], pb[b][:, 0:TT], AF.Tanh, scale=sdh_c[:, j:j + 1]),
                    deps=[mmt] + sl_sgm[ssl_t].wdeps() + init_small)
                sl_sgm[ssl_t].wrote(tt1)
                ty2 = P.op("scalar", lambda e, b=bank, g=gsl: e.activation(
                    gfin_v[g], pb[b][:, 0:TT], AF.Identity),
                    deps=[mmt] + sl_gfin[gsl].wdeps())
                sl_pb[bank].read(ty2)
                sl_gfin[gsl].wrote(ty2)
                tg = P.op("vector", lambda e, g=gsl, t=ssl_t: e.scalar_tensor_tensor(
                    gfin_v[g], sgm_v[t], 1.0, gfin_v[g], ALU.add, ALU.mult),
                    deps=[tt1, ty2])
                sl_sgm[ssl_t].read(tg)
                sl_gfin[gsl].wrote(tg)
                ssl = j % 2
                ts = P.op("sync", lambda e, j=j, s=ssl, tt=tt: e.dma_start(
                    out=sfin_v[s], in_=s_d[j, :, tt * TT:(tt + 1) * TT]),
                    deps=sl_sfin[ssl].wdeps() + (first_deps if tt == 0 and j < 2 else []),
                    dma_sem=dma_sfin[ssl])
                sl_sfin[ssl].wrote(ts)
                tx = P.op("sync", lambda e, j=j, s=ssl, tt=tt: e.dma_start(
                    out=xfin_v[s], in_=x_fm[j * 128:(j + 1) * 128, tt * TT:(tt + 1) * TT]),
                    deps=sl_xfin[ssl].wdeps() + (first_deps if tt == 0 and j < 2 else []),
                    dma_sem=dma_xfin[ssl])
                sl_xfin[ssl].wrote(tx)
                tm = P.op("vector", lambda e, s=ssl, g=gsl, j=j: e.scalar_tensor_tensor(
                    sgm_v[s], gfin_v[g], sdh_c[:, j:j + 1], sfin_v[s], ALU.mult, ALU.mult),
                    deps=[ts, tg] + sl_sgm[ssl].wdeps())
                sl_sfin[ssl].read(tm)
                sl_gfin[gsl].read(tm)
                sl_sgm[ssl].wrote(tm)
                ty = P.op("vector", lambda e, s=ssl: e.tensor_tensor(
                    xfin_v[s], sgm_v[s], xfin_v[s], ALU.add),
                    deps=[tm, tx] + sl_xfin[ssl].rds)
                sl_sgm[ssl].read(ty)
                sl_xfin[ssl].wrote(ty)
                st = P.op("gpsimd", lambda e, j=j, s=ssl, tt=tt: e.dma_start(
                    out=y_fm[j * 128:(j + 1) * 128, tt * TT:(tt + 1) * TT], in_=xfin_v[s]),
                    deps=[ty], dma_sem=dma_y[(tt * J + j) % 4])
                sl_xfin[ssl].read(st)
                y_final.append(st)

        ln_unit(0)

        mi = 0
        ln_defer = {}
        deferred_fuse = []
        for ti in range(NTA):
            quota = (M * (ti + 1)) // NTA
            loads, tts = [], []
            while mi < quota:
                def mk(m):
                    st = {}
                    def load():
                        st['tc'] = ffnA_load(m, [])
                    def tt_run(tt, m=m):
                        if deferred_fuse:
                            deferred_fuse.pop(0)()
                        deferred_fuse.append(ffnA_tt(m, tt, st['tc']))
                    return load, [lambda tt=tt: tt_run(tt) for tt in range(NTT)]
                ld_, tt_ = mk(mi)
                if ti == 0 and mi == 0:
                    for t_ln in range(NTT - 1, 0, -1):
                        tt_.insert(t_ln, lambda t=t_ln: ln_unit(t))
                loads.append(ld_)
                tts.append(tt_)
                mi += 1
            micro = []
            if ti in ln_defer and ln_defer[ti] < NTT:
                micro.append(lambda t=ln_defer[ti]: ln_unit(t))
            if loads:
                micro.append(loads[0])
                if len(loads) > 1:
                    micro.append(loads[1])
            for u in range(len(loads)):
                if u + 2 < len(loads):
                    micro.append(loads[u + 2])
                micro.extend(tts[u])

            th = [lambda g=g: att_pre_group(ti, g) for g in range(12)]
            kst = {}
            def _pre():
                kst['k'] = att_pre(ti)
            th.append(_pre)
            th.append(lambda: att_pair_fm(ti, 0, kst['k']))
            if ti > 0:
                th.append(lambda t=ti - 1: att_proj(t))
            for i in range(CH):
                if i + 1 < CH:
                    th.append(lambda i=i: att_pair_fm(ti, i + 1, kst['k']))
                th.append(lambda i=i: att_pair_tree(ti, i))
                th.append(lambda i=i: att_pair_back(ti, i))
            np_ = len(micro)
            nth = len(th)
            for k, t_ in enumerate(th):
                t_()
                want = (np_ * (k + 1)) // nth
                while len(micro) > np_ - want:
                    micro.pop(0)()
            while micro:
                micro.pop(0)()
        while deferred_fuse:
            deferred_fuse.pop(0)()
        att_proj(NTA - 1)

        fuse_done = [(s, s.val) for s in dma_fst]
        s_done = [(s, s.val) for s in dma_sst]
        ln_done = [sl_xln.wr] + sl_xln.rds + sl_lnrow.rds + ([sl_lnrow.wr] if sl_lnrow.wr else [])
        b_first = fuse_done + s_done + ln_done
        for tt in range(NTT):
            ffnB_unit(tt, b_first if tt == 0 else [])

        final_waits = [(s, s.val) for s in dma_y]

        for scc in P.all_sems:
            scc.handle = ctx.enter_context(nc.semaphore(scc.name))
        block = ctx.enter_context(nc.Block())

        @block.sync
        def _(e):
            P.emit_engine(e, "sync")
            for s2, v2 in final_waits:
                e.wait_ge(s2.handle, v2)

        @block.vector
        def _(e):
            P.emit_engine(e, "vector")

        @block.scalar
        def _(e):
            P.emit_engine(e, "scalar")

        @block.tensor
        def _(e):
            P.emit_engine(e, "tensor")

        @block.gpsimd
        def _(e):
            P.emit_engine(e, "gpsimd")

    return nc


def kernel(**inputs):
    x, shared = host_pack(inputs)
    NTOK = x.shape[0] // N_CORES
    nc = build_program(NTOK)
    in_maps = []
    for i in range(N_CORES):
        m = dict(x_fm=np.ascontiguousarray(x[i * NTOK:(i + 1) * NTOK].T))
        m.update(shared)
        in_maps.append(m)
    res = run_bass_kernel_spmd(nc, in_maps, core_ids=list(range(N_CORES)))
    outs = [np.ascontiguousarray(np.asarray(res.results[i]["y_fm"]).T) for i in range(N_CORES)]
    return np.concatenate(outs, axis=0).reshape(4, 4096, D).astype(np.float32)


# version 39
# speedup vs baseline: 1.0302x; 1.0302x over previous
import numpy as np
import ml_dtypes
import concourse.bass as bass
import concourse.mybir as mybir
from concourse.bass_utils import run_bass_kernel_spmd
from contextlib import ExitStack

F32 = mybir.dt.float32
BF16 = mybir.dt.bfloat16
I8 = mybir.dt.int8
U8 = mybir.dt.uint8
AF = mybir.ActivationFunctionType
ALU = mybir.AluOpType

N_CORES = 8
D = 2048
CH = 16
NC = 32
M = 64
J = 16
EPS = 1e-5

ENGINES = ("sync", "vector", "scalar", "tensor", "gpsimd")


class SemCtr:
    _n = 0

    def __init__(self, name=None):
        self.name = name or f"sc{SemCtr._n}"
        SemCtr._n += 1
        self.val = 0
        self.handle = None


class Prog:

    def __init__(self):
        self.ops = {e: [] for e in ENGINES}
        self.ectr = {e: SemCtr("ctr_" + e) for e in ENGINES}
        self.watermark = {e: {} for e in ENGINES}
        self.all_sems = list(self.ectr.values())

    def new_sem(self, name=None):
        s = SemCtr(name)
        self.all_sems.append(s)
        return s

    def op(self, eng, emit, deps=(), dma_sem=None, noinc=False):
        sc = dma_sem if dma_sem is not None else self.ectr[eng]
        step = 16 if dma_sem is not None else 1
        if dma_sem is None and self.ectr[eng].val > 0:
            deps = list(deps) + [(self.ectr[eng], self.ectr[eng].val)]
        if noinc:
            best = {}
            for d in deps:
                if d is None:
                    continue
                s2, v2 = d
                if v2 > best.get(id(s2), (None, 0))[1]:
                    best[id(s2)] = (s2, v2)
            needed = []
            wm = self.watermark[eng]
            for s2, v2 in best.values():
                if v2 > wm.get(id(s2), 0):
                    needed.append((s2, v2))
                    wm[id(s2)] = v2
            self.ops[eng].append((emit, needed, None, 0))
            return None
        sc.val += step
        tok = (sc, sc.val)
        best = {}
        for d in deps:
            if d is None:
                continue
            s2, v2 = d
            if v2 > best.get(id(s2), (None, 0))[1]:
                best[id(s2)] = (s2, v2)
        needed = []
        wm = self.watermark[eng]
        for s2, v2 in best.values():
            if v2 > wm.get(id(s2), 0):
                needed.append((s2, v2))
                wm[id(s2)] = v2
        self.ops[eng].append((emit, needed, sc, step))
        return tok

    def emit_engine(self, eng_obj, eng_name, tail=None):
        for emit, needed, sc, step in self.ops[eng_name]:
            for (s2, v2) in needed[:-1]:
                eng_obj.wait_ge(s2.handle, v2)
            inst = emit(eng_obj)
            if needed:
                inst._wait_ge(needed[-1][0].handle, needed[-1][1])
            if sc is not None:
                inst.then_inc(sc.handle, step)
        if tail is not None:
            tail(eng_obj)


class Slot:
    def __init__(self):
        self.wr = None
        self.rds = []

    def wdeps(self):
        return ([self.wr] if self.wr else []) + list(self.rds)

    def wrote(self, tok):
        self.wr = tok
        self.rds = []

    def rdeps(self):
        return [self.wr] if self.wr else []

    def read(self, tok):
        self.rds.append(tok)


def host_pack(inputs):
    x = np.ascontiguousarray(np.asarray(inputs["L0_attn_x"], dtype=np.float32).reshape(-1, D))
    Wqkv = np.asarray(inputs["L0_attn_Wqkv"], dtype=np.float32)
    bqkv = np.asarray(inputs["L0_attn_bqkv"], dtype=np.float32)
    Wout = np.asarray(inputs["L0_attn_Wout"], dtype=np.float32)
    bout = np.asarray(inputs["L0_attn_bout"], dtype=np.float32)
    wg = np.asarray(inputs["L0_ffn_w_gate"])
    wu = np.asarray(inputs["L0_ffn_w_up"])
    wd = np.asarray(inputs["L0_ffn_w_down"])
    sg = np.asarray(inputs["L0_ffn_s_gate"], dtype=np.float32)
    su = np.asarray(inputs["L0_ffn_s_up"], dtype=np.float32)
    sd = np.asarray(inputs["L0_ffn_s_down"], dtype=np.float32)

    wqkv_bd = np.zeros((CH, 3, 128, 128), dtype=np.float32)
    for ch in range(CH):
        for s in range(3):
            wqkv_bd[ch, s, 0:64, 0:64] = Wqkv[2 * ch, :, s * 64:(s + 1) * 64]
            wqkv_bd[ch, s, 64:128, 64:128] = Wqkv[2 * ch + 1, :, s * 64:(s + 1) * 64]
    wqkv_bd = np.ascontiguousarray(wqkv_bd.reshape(CH * 3 * 128, 128))

    p = np.arange(128)
    chn = np.arange(CH)
    bq_col = np.ascontiguousarray(bqkv[(2 * chn[None, :] + p[:, None] // 64), (p[:, None] % 64)])
    bk_col = np.ascontiguousarray(bqkv[(2 * chn[None, :] + p[:, None] // 64), 64 + (p[:, None] % 64)])
    bv_col = np.ascontiguousarray(bqkv[(2 * chn[None, :] + p[:, None] // 64), 128 + (p[:, None] % 64)])

    wout_bd = np.zeros((128, 128), dtype=np.float32)
    wout_bd[0:64, 0:64] = Wout
    wout_bd[64:128, 64:128] = Wout
    bout_col = np.ascontiguousarray(bout[p % 64].reshape(128, 1))

    wg4 = wg.reshape(CH, 128, M, 128).transpose(2, 1, 0, 3)
    wu4 = wu.reshape(CH, 128, M, 128).transpose(2, 1, 0, 3)
    wgu_pk = np.ascontiguousarray(np.stack([wg4, wu4], axis=2).reshape(M, 128, 2 * CH * 128))
    wd_pk = np.ascontiguousarray(wd.reshape(M, 128, J, 128).transpose(2, 1, 0, 3).reshape(J, 128, M * 128))

    sg_col = np.ascontiguousarray(sg.reshape(M, 128).T)
    su_col = np.ascontiguousarray(su.reshape(M, 128).T)
    sd_col = np.ascontiguousarray(sd.reshape(J, 128).T)

    return x, dict(wqkv_bd=wqkv_bd, bq_col=bq_col, bk_col=bk_col, bv_col=bv_col,
                   wout_bd=wout_bd, bout_col=bout_col, wgu_pk=wgu_pk, wd_pk=wd_pk,
                   sg_col=sg_col, su_col=su_col, sd_col=sd_col)


def build_program(NTOK):
    TT = 512
    TA = 128
    NTT = NTOK // TT
    NTA = NTOK // TA

    nc = bass.Bass("TRN2", target_bir_lowering=False)
    P = Prog()

    x_fm = nc.dram_tensor("x_fm", [D, NTOK], F32, kind="ExternalInput").ap()
    wqkv_in = nc.dram_tensor("wqkv_bd", [CH * 3 * 128, 128], F32, kind="ExternalInput").ap()
    bq_in = nc.dram_tensor("bq_col", [128, CH], F32, kind="ExternalInput").ap()
    bk_in = nc.dram_tensor("bk_col", [128, CH], F32, kind="ExternalInput").ap()
    bv_in = nc.dram_tensor("bv_col", [128, CH], F32, kind="ExternalInput").ap()
    wout_in = nc.dram_tensor("wout_bd", [128, 128], F32, kind="ExternalInput").ap()
    bout_in = nc.dram_tensor("bout_col", [128, 1], F32, kind="ExternalInput").ap()
    wgu_in = nc.dram_tensor("wgu_pk", [M, 128, 2 * CH * 128], I8, kind="ExternalInput").ap()
    wd_in = nc.dram_tensor("wd_pk", [J, 128, M * 128], I8, kind="ExternalInput").ap()
    sg_in = nc.dram_tensor("sg_col", [128, M], F32, kind="ExternalInput").ap()
    su_in = nc.dram_tensor("su_col", [128, M], F32, kind="ExternalInput").ap()
    sd_in = nc.dram_tensor("sd_col", [128, J], F32, kind="ExternalInput").ap()
    y_fm = nc.dram_tensor("y_fm", [D, NTOK], F32, kind="ExternalOutput").ap()

    fuse_d = nc.dram_tensor("fuse_d", [M, 128, NTOK], BF16, kind="Internal").ap()
    s_d = nc.dram_tensor("s_d", [J, 128, NTOK], BF16, kind="Internal").ap()

    with ExitStack() as ctx:
        def sb(name, shape, dt):
            return ctx.enter_context(nc.sbuf_tensor(name, shape, dt))

        ones_col = sb("ones_col", [128, 1], F32)
        eps_col = sb("eps_col", [128, 1], F32)
        ones_row = sb("ones_row", [1, 128], F32)
        wqkv_b = sb("wqkv_b", [128, CH * 3 * 128], BF16)
        wout_b = sb("wout_b", [128, 128], BF16)
        bq8 = sb("bq8", [128, CH], F32)
        bk_c = sb("bk_c", [128, CH], F32)
        bv_c = sb("bv_c", [128, CH], F32)
        bout_c = sb("bout_c", [128, 1], F32)
        sg_c = sb("sg_c", [128, M], F32)
        su_c = sb("su_c", [128, M], F32)
        sd_c = sb("sd_c", [128, J], F32)

        xn = sb("xn", [128, max(CH * NTOK, M * TT)], BF16)

        af32 = sb("af32", [128, 11264], F32)
        xln_v = af32[:, 0:8192]
        lnrow = af32[:, 8192:10240]
        sq_v = [af32[:, 10240 + i * 512:10240 + (i + 1) * 512] for i in range(2)]
        xfin_v = [af32[:, 8192 + i * 512:8192 + (i + 1) * 512] for i in range(2)]

        q_sb2 = sb("q_sb2", [128, 2 * CH * TA], BF16)
        k_sb2 = sb("k_sb2", [128, 2 * CH * TA], BF16)
        v_sb2 = sb("v_sb2", [128, 2 * CH * TA], BF16)
        qd_s = sb("qd_s", [128, TA], BF16)
        E_s = sb("E_s", [128, 4 * CH * TA], BF16)
        zt1 = sb("zt1", [128, 8 * TA], BF16)
        kmx = zt1[:, 3 * TA:4 * TA]
        Zall = sb("Zall", [128, 4 * TA], BF16)
        u_dup = zt1[:, 4 * TA:5 * TA]
        ctx_s = sb("ctx_s", [128, CH * TA], BF16)
        s_stage = sb("s_stage", [128, 4 * TA], BF16)
        hfold = zt1[:, 2 * TA:3 * TA]

        abf = sb("abf", [128, 11264], BF16)
        wgu3 = sb("wgu3", [128, 4096], BF16)
        wgubf_v = [abf[:, 0:4096], abf[:, 4096:8192], wgu3[:, :]]
        g_v = [abf[:, 8192 + i * 512:8192 + (i + 1) * 512] for i in range(2)]
        uu_v = [abf[:, 9216 + i * 512:9216 + (i + 1) * 512] for i in range(2)]
        fuse_st_v = [abf[:, 10240 + i * 512:10240 + (i + 1) * 512] for i in range(2)]
        wdq_v = [abf[:, i * 2048:(i + 1) * 2048] for i in range(3)]
        sfin_v = [abf[:, 6144 + i * 512:6144 + (i + 1) * 512] for i in range(2)]
        gfin_v = [abf[:, 7168 + i * 512:7168 + (i + 1) * 512] for i in range(2)]
        sgm_v = [abf[:, 8192 + i * 512:8192 + (i + 1) * 512] for i in range(2)]

        fuse_tt_v = xn[:, 0:M * TT]

        pb = [ctx.enter_context(nc.psum_tensor(f"pb{i}", [128, 512], F32)) for i in range(8)]

        sl_xln = Slot()
        sl_lnrow = Slot()
        sl_sq = [Slot(), Slot()]
        sl_q2 = [Slot(), Slot()]
        sl_k2 = [Slot(), Slot()]
        sl_v2 = [Slot(), Slot()]
        sl_qd = [Slot(), Slot()]
        sl_kmx = Slot()
        sl_E = [Slot(), Slot(), Slot(), Slot()]
        sl_zt1, sl_Z, sl_ud = Slot(), Slot(), Slot()
        sl_ctx = Slot()
        sl_hf = Slot()
        _ss = Slot()
        sl_sst = [_ss, _ss]
        sl_wgubf = [Slot(), Slot(), Slot()]
        sl_g = [Slot(), Slot()]
        sl_uu = [Slot(), Slot()]
        sl_fst = [Slot(), Slot()]
        sl_pb = [Slot() for _ in range(8)]
        sl_wdq = [Slot(), Slot(), Slot()]
        sl_sfin = [Slot(), Slot()]
        sl_xfin = [Slot(), Slot()]
        sl_gfin = [Slot(), Slot()]
        sl_sgm = [Slot(), Slot()]
        sl_fuse_tt = Slot()
        sl_misc = Slot()

        dma_init = P.new_sem("d_init")
        dma_init2 = [P.new_sem(f"d_init2_{i}") for i in range(7)]
        dma_x = P.new_sem("d_x")
        dma_wgu = [P.new_sem(f"d_wgu{i}") for i in range(3)]
        dma_fst = [P.new_sem(f"d_fst{i}") for i in range(4)]
        dma_sst = [P.new_sem(f"d_sst{i}") for i in range(2)]
        dma_ftt = [P.new_sem(f"d_ftt{i}") for i in range(4)]
        dma_wdq = [P.new_sem(f"d_wdq{i}") for i in range(3)]
        dma_sfin = [P.new_sem(f"d_sfin{i}") for i in range(2)]
        dma_xfin = [P.new_sem(f"d_xfin{i}") for i in range(2)]
        dma_y = [P.new_sem(f"d_y{i}") for i in range(4)]

        cast_toks = []
        for rnd in range(3):
            t = P.op("sync", lambda e, r=rnd: e.dma_start(
                out=lnrow[:, :].rearrange("p (c n) -> p c n", c=16),
                in_=wqkv_in.rearrange("(c p) n -> p c n", p=128)[:, r * 16:(r + 1) * 16, :]),
                deps=sl_lnrow.wdeps(), dma_sem=dma_init)
            sl_lnrow.wrote(t)
            for ci in range(16):
                blk = rnd * 16 + ci
                sc = 0.125 if blk % 3 == 0 else 1.0
                tc = P.op("scalar", lambda e, b=blk, ci=ci, sc=sc: e.activation(
                    wqkv_b[:, b * 128:(b + 1) * 128], lnrow[:, ci * 128:(ci + 1) * 128],
                    AF.Copy, scale=sc), deps=[t])
                sl_lnrow.read(tc)
                cast_toks.append(tc)
        ld = []
        for li, (dst, src) in enumerate([(bq8, bq_in), (bk_c, bk_in), (bv_c, bv_in),
                           (bout_c, bout_in), (sg_c, sg_in), (su_c, su_in), (sd_c, sd_in)]):
            ld.append(P.op("sync", lambda e, d=dst, s=src: e.dma_start(out=d[:], in_=s),
                           dma_sem=dma_init2[li]))
        t_oc = P.op("vector", lambda e: e.memset(ones_col[:], 1.0))
        t_eps = P.op("vector", lambda e: e.memset(eps_col[:], float(EPS)))
        t_or = P.op("vector", lambda e: e.memset(ones_row[:], 1.0))
        tb = P.op("vector", lambda e: e.tensor_scalar_mul(bq8[:], bq8[:], 0.125), deps=ld)
        tb2 = P.op("vector", lambda e: e.tensor_scalar_mul(sg_c[:], sg_c[:], 0.5), deps=ld)
        tb3 = P.op("vector", lambda e: e.tensor_tensor(su_c[:], su_c[:], sg_c[:], ALU.mult), deps=[tb2])
        tb4 = P.op("vector", lambda e: e.tensor_scalar_mul(sd_c[:], sd_c[:], 0.5), deps=ld)
        sgh_c, suf_c, sdh_c = sg_c, su_c, sd_c
        t = P.op("sync", lambda e: e.dma_start(out=lnrow[:, 0:128], in_=wout_in),
                 deps=sl_lnrow.wdeps(), dma_sem=dma_init)
        sl_lnrow.wrote(t)
        tw = P.op("scalar", lambda e: e.activation(wout_b[:], lnrow[:, 0:128], AF.Copy), deps=[t])
        sl_lnrow.read(tw)
        sl_misc.wrote(tw)
        init_small = [(sm, sm.val) for sm in dma_init2] + [tb, tb3, tb4]

        xn_ready = [None] * NTT

        def ln_unit(tt):
            src = x_fm.rearrange("(c p) n -> p c n", p=128)[:, :, tt * TT:(tt + 1) * TT]
            dl = P.op("sync", lambda e: e.dma_start(
                out=xln_v.rearrange("p (c n) -> p c n", c=CH), in_=src),
                deps=sl_xln.wdeps(), dma_sem=dma_x)
            sl_xln.wrote(dl)
            mean_ps, sq_ps, rrep_ps, mrep_ps = pb[4], pb[5], pb[6], pb[7]
            tmean = tsq = None
            for chi in range(CH):
                xc = xln_v[:, chi * TT:(chi + 1) * TT]
                tmean = P.op("tensor", lambda e, xc=xc, f=(chi == 0), l=(chi == CH - 1): e.matmul(
                    mean_ps[0:1, 0:TT], ones_col[:], xc, start=f, stop=l),
                    deps=([dl, t_oc] + sl_pb[4].wdeps()) if chi == 0 else [],
                    noinc=(chi < CH - 1))
                ssl = chi % 2
                tq = P.op("scalar", lambda e, xc=xc, s=ssl: e.activation(sq_v[s][:, 0:TT], xc, AF.Square),
                          deps=[dl] + sl_sq[ssl].wdeps())
                sl_sq[ssl].wrote(tq)
                tsq = P.op("tensor", lambda e, f=(chi == 0), l=(chi == CH - 1), s=ssl: e.matmul(
                    sq_ps[0:1, 0:TT], ones_col[:], sq_v[s][:, 0:TT], start=f, stop=l),
                    deps=[tq] + (sl_pb[5].wdeps() if chi == 0 else []))
                sl_sq[ssl].read(tsq)
            sl_pb[4].wrote(tmean)
            sl_pb[5].wrote(tsq)
            r_mean = lnrow[0:1, 0:TT]
            r_m2 = lnrow[0:1, TT:2 * TT]
            r_var = lnrow[0:1, 2 * TT:3 * TT]
            r_r = lnrow[0:1, 3 * TT:4 * TT]
            t1 = P.op("vector", lambda e: e.tensor_scalar_mul(r_mean, mean_ps[0:1, 0:TT], 1.0 / D),
                      deps=[tmean] + sl_lnrow.wdeps())
            sl_pb[4].read(t1)
            t2 = P.op("vector", lambda e: e.tensor_tensor(r_m2, r_mean, r_mean, ALU.mult), deps=[t1])
            t3 = P.op("vector", lambda e: e.tensor_scalar_mul(r_var, sq_ps[0:1, 0:TT], 1.0 / D),
                      deps=[tsq])
            sl_pb[5].read(t3)
            t4 = P.op("vector", lambda e: e.tensor_tensor(r_var, r_var, r_m2, ALU.subtract), deps=[t2, t3])
            t5 = P.op("scalar", lambda e: e.activation(r_m2, r_var, AF.Ln, bias=eps_col[0:1, :]), deps=[t4, t_eps])
            t6 = P.op("scalar", lambda e: e.activation(r_r, r_m2, AF.Exp, scale=-0.5), deps=[t5])
            sl_lnrow.wrote(t6)
            t7 = P.op("tensor", lambda e: e.matmul(rrep_ps[:, 0:TT], ones_row[:], r_r, start=True, stop=True),
                      deps=[t6, t_or] + sl_pb[6].wdeps())
            sl_pb[6].wrote(t7)
            sl_lnrow.read(t7)
            t8 = P.op("tensor", lambda e: e.matmul(mrep_ps[:, 0:TT], ones_row[:], r_mean, start=True, stop=True),
                      deps=[t6] + sl_pb[7].wdeps())
            sl_pb[7].wrote(t8)
            sl_lnrow.read(t8)
            last = None
            for chi in range(CH):
                xc = xln_v[:, chi * TT:(chi + 1) * TT]
                xno = xn[:, chi * NTOK + tt * TT: chi * NTOK + (tt + 1) * TT]
                ta = P.op("vector", lambda e, xc=xc, o=xno: e.tensor_tensor(o, xc, mrep_ps[:, 0:TT], ALU.subtract),
                          deps=[t8, dl])
                tbm = P.op("vector", lambda e, o=xno: e.tensor_tensor(o, o, rrep_ps[:, 0:TT], ALU.mult),
                           deps=[t7])
                last = tbm
                sl_pb[6].read(tbm)
                sl_pb[7].read(ta)
                sl_xln.read(tbm)
            xn_ready[tt] = last
            return last

        s_store_toks = []

        def att_pre_group(ti, gi):
            t0 = ti * TA
            po = (ti % 2) * CH * TA
            dep_xn = [xn_ready[t0 // TT]]
            outs = {0: q_sb2, 1: k_sb2, 2: v_sb2}
            slo = {0: sl_q2[ti % 2], 1: sl_k2[ti % 2], 2: sl_v2[ti % 2]}
            items = [(s, chi) for s in range(3) for chi in range(CH)]
            four = items[gi * 4:(gi + 1) * 4]
            bank = gi % 2
            s = four[0][0]
            ch0 = four[0][1]
            mmt = None
            for li, (s_, chi) in enumerate(four):
                woff = (chi * 3 + s_) * 128
                rv = xn[:, chi * NTOK + t0: chi * NTOK + t0 + TA]
                dep = (dep_xn + sl_pb[bank].wdeps() + sl_misc.rdeps()) if li == 0 else []
                mmt = P.op("tensor", lambda e, w=woff, rv=rv, li=li, b=bank: e.matmul(
                    pb[b][:, li * TA:(li + 1) * TA], wqkv_b[:, w:w + 128], rv,
                    start=True, stop=True), deps=dep, noinc=(li < 3))
            sl_pb[bank].wrote(mmt)
            ev = P.op("scalar", lambda e, b=bank, d=outs[s], c0=ch0: e.activation(
                d[:, po + c0 * TA:po + (c0 + 4) * TA], pb[b][:, 0:4 * TA], AF.Copy),
                deps=[mmt] + slo[s].wdeps())
            sl_pb[bank].read(ev)
            slo[s].rds.append(ev)
            if gi == 11:
                for s2 in range(3):
                    slo[s2].wrote(slo[s2].rds[-1])

        def att_pre(ti):
            t0 = ti * TA
            po = (ti % 2) * CH * TA
            k_sb = k_sb2[:, po:po + CH * TA]
            sl_k = sl_k2[ti % 2]
            tkp = None
            for chi in range(CH):
                kv = k_sb[:, chi * TA:(chi + 1) * TA]
                tkp = P.op("vector", lambda e, kv=kv, c=chi: e.tensor_scalar(
                    kv, kv, bk_c[:, c:c + 1], None, ALU.add),
                    deps=(sl_k.wdeps() + sl_k.rdeps() + init_small) if chi == 0 else [])
            sl_k.wrote(tkp)
            return k_sb[:, :].rearrange("p (j n) -> p j n", j=CH)

        pair_st = {}

        qd2 = zt1[:, 6 * TA:8 * TA]

        def att_pair_fm(ti, i, kall3):
            po = (ti % 2) * CH * TA
            q_sb = q_sb2[:, po:po + CH * TA]
            k_sb = k_sb2[:, po:po + CH * TA]
            sl_q = sl_q2[ti % 2]
            sl_k = sl_k2[ti % 2]
            sl0 = (i % 2) * 2
            tq = None
            for h in (0, 1):
                qsrc = q_sb[64 * h:64 * (h + 1), i * TA:(i + 1) * TA]
                bqv = bq8[64 * h:64 * h + 64, i:i + 1]
                tq1 = P.op("vector", lambda e, qs=qsrc, b=bqv, h=h: e.tensor_scalar(
                    qd2[0:64, h * TA:(h + 1) * TA], qs, b, None, ALU.add),
                    deps=sl_q.rdeps() + sl_qd[h].wdeps() + init_small + sl_zt1.wdeps())
                tq = P.op("vector", lambda e, qs=qsrc, b=bqv, h=h: e.tensor_scalar(
                    qd2[64:128, h * TA:(h + 1) * TA], qs, b, None, ALU.add), deps=[])
                sl_q.read(tq)
                sl_qd[h].wrote(tq)
                sl_zt1.read(tq)
            ev2 = E_s[:, sl0 * CH * TA:(sl0 + 2) * CH * TA]
            ev4 = ev2.rearrange("p (h j n) -> p h j n", h=2, j=CH)
            qb4 = qd2.rearrange("p (h n) -> p h n", h=2).unsqueeze(2).broadcast_to([128, 2, CH, TA])
            kb4 = kall3.unsqueeze(1).broadcast_to([128, 2, CH, TA]) if False else None
            k4 = k_sb[:, :].rearrange("p (j n) -> p j n", j=CH).unsqueeze(1).broadcast_to([128, 2, CH, TA])
            tE = P.op("vector", lambda e: e.tensor_tensor(ev4, qb4, k4, ALU.mult),
                      deps=sl_k.rdeps() + sl_E[sl0].wdeps() + sl_E[sl0 + 1].wdeps() + [tq])
            sl_k.read(tE)
            sl_qd[0].read(tE)
            sl_qd[1].read(tE)
            sl_zt1.read(tE)
            tX = P.op("scalar", lambda e, ev=ev2: e.activation(ev, ev, AF.Exp), deps=[tE])
            sl_E[sl0].wrote(tX)
            sl_E[sl0 + 1].wrote(tX)
            pair_st[(ti, i)] = [tX, tX]

        def att_pair_tree(ti, i):
            exp_toks = pair_st.pop((ti, i))
            tz_pair = []
            for h in (0, 1):
                sl = (i % 2) * 2 + h
                ev = E_s[:, sl * CH * TA:(sl + 1) * CH * TA]
                tz = P.op("vector", lambda e, ev=ev: e.tensor_tensor(
                    zt1[:, 0:8 * TA], ev[:, 0:8 * TA], ev[:, 8 * TA:16 * TA], ALU.add),
                    deps=[exp_toks[h]] + sl_zt1.wdeps())
                tz = P.op("vector", lambda e: e.tensor_tensor(
                    zt1[:, 0:4 * TA], zt1[:, 0:4 * TA], zt1[:, 4 * TA:8 * TA], ALU.add), deps=[tz])
                tz = P.op("vector", lambda e: e.tensor_tensor(
                    zt1[:, 0:2 * TA], zt1[:, 0:2 * TA], zt1[:, 2 * TA:4 * TA], ALU.add), deps=[tz])
                tz = P.op("vector", lambda e: e.tensor_tensor(
                    zt1[:, 0:TA], zt1[:, 0:TA], zt1[:, TA:2 * TA], ALU.add), deps=[tz])
                zdst = Zall[64 * h:64 * h + 64, (i % 4) * TA:(i % 4 + 1) * TA]
                tz = P.op("vector", lambda e: e.tensor_copy(hfold[0:64, :], zt1[64:128, 0:TA]),
                          deps=[tz] + sl_hf.wdeps())
                sl_hf.wrote(tz)
                tz = P.op("vector", lambda e, zd=zdst: e.tensor_tensor(
                    zd, zt1[0:64, 0:TA], hfold[0:64, :], ALU.add),
                    deps=[tz] + sl_Z.wdeps())
                sl_hf.read(tz)
                sl_zt1.wrote(tz)
                sl_Z.rds.append(tz)
                tz_pair.append(tz)
            zcol = Zall[:, (i % 4) * TA:(i % 4 + 1) * TA]
            def _recip(e, zc=zcol):
                with nc.allow_low_precision(reason="1/Z on bf16 softmax denominator"):
                    return e.reciprocal(zc, zc)
            tr = P.op("vector", _recip, deps=tz_pair)
            sl_Z.rds.append(tr)
            pair_st[(ti, i, "r")] = tr

        u_dup2 = zt1[:, 4 * TA:6 * TA]

        def att_pair_back(ti, i):
            po = (ti % 2) * CH * TA
            v_sb = v_sb2[:, po:po + CH * TA]
            sl_v = sl_v2[ti % 2]
            tr = pair_st.pop((ti, i, "r"))
            sl0 = (i % 2) * 2
            tu = None
            for h in (0, 1):
                vsrc = v_sb[64 * h:64 * (h + 1), i * TA:(i + 1) * TA]
                rzv = Zall[64 * h:64 * h + 64, (i % 4) * TA:(i % 4 + 1) * TA]
                bvv = bv_c[64 * h:64 * h + 64, i:i + 1]
                tu1 = P.op("vector", lambda e, vs=vsrc, rz=rzv, b=bvv, h=h: e.scalar_tensor_tensor(
                    u_dup2[0:64, h * TA:(h + 1) * TA], vs, b, rz, ALU.add, ALU.mult),
                    deps=sl_v.rdeps() + [tr] + sl_ud.wdeps() + init_small)
                tu = P.op("vector", lambda e, vs=vsrc, rz=rzv, b=bvv, h=h: e.scalar_tensor_tensor(
                    u_dup2[64:128, h * TA:(h + 1) * TA], vs, b, rz, ALU.add, ALU.mult), deps=[])
                sl_v.read(tu)
                sl_Z.rds.append(tu)
            sl_ud.wrote(tu)
            ev2 = E_s[:, sl0 * CH * TA:(sl0 + 2) * CH * TA]
            ev4 = ev2.rearrange("p (h j n) -> p h j n", h=2, j=CH)
            ub4 = u_dup2.rearrange("p (h n) -> p h n", h=2).unsqueeze(2).broadcast_to([128, 2, CH, TA])
            tt_ = P.op("vector", lambda e: e.tensor_tensor(ev4, ub4, ev4, ALU.mult),
                       deps=[tu] + sl_E[sl0].rdeps() + sl_E[sl0 + 1].rdeps())
            sl_ud.read(tt_)
            sl_zt1.read(tt_)
            sl_E[sl0].wrote(tt_)
            sl_E[sl0 + 1].wrote(tt_)
            for h in (0, 1):
                ev = E_s[:, (sl0 + h) * CH * TA:(sl0 + h + 1) * CH * TA]
                if i == 0 and h == 0:
                    tc = P.op("vector", lambda e, ev=ev: e.tensor_copy(ctx_s[:], ev),
                              deps=[tt_] + sl_ctx.wdeps())
                else:
                    tc = P.op("vector", lambda e, ev=ev: e.tensor_tensor(ctx_s[:], ctx_s[:], ev, ALU.add),
                              deps=[tt_])
                sl_E[sl0 + h].read(tc)
                sl_ctx.wrote(tc)

        def att_proj(ti):
            t0 = ti * TA
            sl_Z.wrote(sl_Z.rds[-1])
            for g4 in range(4):
                bank = g4 % 2
                mmt = None
                for li in range(4):
                    jj = g4 * 4 + li
                    rv = ctx_s[:, jj * TA:(jj + 1) * TA]
                    dep = ([sl_ctx.wr] + sl_pb[bank].wdeps() + sl_misc.rdeps()) if li == 0 else []
                    mmt = P.op("tensor", lambda e, rv=rv, li=li, b=bank: e.matmul(
                        pb[b][:, li * TA:(li + 1) * TA], wout_b[:], rv, start=True, stop=True),
                        deps=dep, noinc=(li < 3))
                sl_pb[bank].wrote(mmt)
                sl_ctx.read(mmt)
                ssl = g4 % 2
                ev = P.op("scalar", lambda e, b=bank, s=ssl: e.activation(
                    s_stage[:, 0:4 * TA], pb[b][:, 0:4 * TA], AF.Identity,
                    bias=bout_c[:, :]),
                    deps=[mmt] + sl_sst[ssl].wdeps() + init_small)
                sl_pb[bank].read(ev)
                sl_sst[ssl].wrote(ev)
                sdst = s_d[g4 * 4:(g4 + 1) * 4, :, t0:t0 + TA].rearrange("j p n -> p j n")
                ssrc = s_stage[:, 0:4 * TA].rearrange("p (j n) -> p j n", j=4)
                st = P.op("scalar", lambda e, d=sdst, s=ssrc: e.dma_start(out=d, in_=s),
                          deps=[ev], dma_sem=dma_sst[ssl])
                sl_sst[ssl].read(st)
                s_store_toks.append(st)

        def ffnA_load(m, extra_dep):
            slot = m % 3
            dl = P.op("gpsimd", lambda e, m=m, s=slot: e.dma_start(
                out=wgubf_v[s], in_=wgu_in[m]),
                deps=sl_wgubf[slot].wdeps() + extra_dep, dma_sem=dma_wgu[slot])
            sl_wgubf[slot].wrote(dl)
            return dl

        def ffnA_tt(m, tt, tc):
            slot = m % 3
            if True:
                u_ = m * NTT + tt
                gbank = (4, 5, 2)[u_ % 3]
                ubank = (6, 7, 3)[u_ % 3]
                mmg = mmu = None
                for chi in range(CH):
                    rv = xn[:, chi * NTOK + tt * TT: chi * NTOK + (tt + 1) * TT]
                    w_g = wgubf_v[slot][:, chi * 128:(chi + 1) * 128]
                    w_u = wgubf_v[slot][:, (CH + chi) * 128:(CH + chi + 1) * 128]
                    mmg = P.op("tensor", lambda e, w=w_g, rv=rv, f=(chi == 0), l=(chi == CH - 1), b=gbank: e.matmul(
                        pb[b][:, 0:TT], w, rv, start=f, stop=l),
                        deps=([tc, xn_ready[tt]] + sl_pb[gbank].wdeps()) if chi == 0 else [],
                        noinc=(chi < CH - 1))
                    mmu = P.op("tensor", lambda e, w=w_u, rv=rv, f=(chi == 0), l=(chi == CH - 1), b=ubank: e.matmul(
                        pb[b][:, 0:TT], w, rv, start=f, stop=l),
                        deps=sl_pb[ubank].wdeps() if chi == 0 else [],
                        noinc=(chi < CH - 1))
                sl_pb[gbank].wrote(mmg)
                sl_pb[ubank].wrote(mmu)
                sl_wgubf[slot].read(mmu)
                gsl = tt % 2
                fsl = tt % 2
                tg = P.op("scalar", lambda e, b=gbank, m=m, g=gsl: e.activation(
                    g_v[g], pb[b][:, 0:TT], AF.Tanh, scale=sgh_c[:, m:m + 1]),
                    deps=[mmg] + sl_g[gsl].wdeps() + init_small)
                sl_g[gsl].wrote(tg)
                ty = P.op("scalar", lambda e, b=gbank, f=fsl: e.activation(
                    fuse_st_v[f], pb[b][:, 0:TT], AF.Identity),
                    deps=[mmg] + sl_fst[fsl].wdeps())
                sl_pb[gbank].read(ty)
                sl_fst[fsl].wrote(ty)
                tu = P.op("scalar", lambda e, b=ubank, m=m, g=gsl: e.activation(
                    uu_v[g], pb[b][:, 0:TT], AF.Identity, scale=suf_c[:, m:m + 1]),
                    deps=[mmu] + sl_uu[gsl].wdeps() + init_small)
                sl_pb[ubank].read(tu)
                sl_uu[gsl].wrote(tu)
                def fuse_math(m=m, tt=tt, gsl=gsl, fsl=fsl, tg=tg, ty=ty, tu=tu):
                    tgr = P.op("vector", lambda e, g=gsl, f=fsl: e.scalar_tensor_tensor(
                        fuse_st_v[f], g_v[g], 1.0, fuse_st_v[f], ALU.add, ALU.mult),
                        deps=[tg, ty])
                    sl_g[gsl].read(tgr)
                    sl_fst[fsl].wrote(tgr)
                    tfu = P.op("vector", lambda e, g=gsl, f=fsl: e.tensor_tensor(
                        fuse_st_v[f], fuse_st_v[f], uu_v[g], ALU.mult),
                        deps=[tgr, tu])
                    sl_uu[gsl].read(tfu)
                    sl_fst[fsl].wrote(tfu)
                    st = P.op("sync", lambda e, f=fsl: e.dma_start(
                        out=fuse_d[m, :, tt * TT:(tt + 1) * TT], in_=fuse_st_v[f]),
                        deps=[tfu], dma_sem=dma_fst[(m % 2) * 2 + (tt % 2)])
                    sl_fst[fsl].read(st)
                return fuse_math

        y_final = []

        def ffnB_unit(tt, first_deps):
            fl = []
            for q in range(4):
                src = fuse_d[q * 16:(q + 1) * 16, :, tt * TT:(tt + 1) * TT].rearrange("m p n -> p m n")
                dst = fuse_tt_v[:, q * 16 * TT:(q + 1) * 16 * TT].rearrange("p (m n) -> p m n", m=16)
                t = P.op("sync", lambda e, s=src, d=dst: e.dma_start(out=d, in_=s),
                         deps=sl_fuse_tt.wdeps() + first_deps if q == 0 else [],
                         dma_sem=dma_ftt[q])
                fl.append(t)
            sl_fuse_tt.wrote(fl[-1])
            sl_fuse_tt.rds = fl[:-1]
            qseq = [(j, q) for j in range(J) for q in range(4)]
            pend = {}

            def wd_load(idx):
                j, q = qseq[idx]
                wsl = idx % 3
                tl = P.op("gpsimd", lambda e, j=j, q=q, w=wsl: e.dma_start(
                    out=wdq_v[w], in_=wd_in[j][:, q * 2048:(q + 1) * 2048]),
                    deps=sl_wdq[wsl].wdeps() + (first_deps if tt == 0 and idx < 3 else []),
                    dma_sem=dma_wdq[wsl])
                sl_wdq[wsl].wrote(tl)
                pend[idx] = (wsl, tl)

            for idx in range(3):
                wd_load(idx)
            for j in range(J):
                bank = 4 + j % 4
                mmt = None
                for q in range(4):
                    idx = j * 4 + q
                    wsl, tl = pend.pop(idx)
                    for mi in range(16):
                        m = q * 16 + mi
                        w = wdq_v[wsl][:, mi * 128:(mi + 1) * 128]
                        rv = fuse_tt_v[:, m * TT:(m + 1) * TT]
                        dep = []
                        if m == 0:
                            dep = [tl, fl[q]] + sl_pb[bank].wdeps()
                        elif mi == 0:
                            dep = [tl, fl[q]]
                        mmt = P.op("tensor", lambda e, w=w, rv=rv, f=(m == 0), l=(m == M - 1), b=bank: e.matmul(
                            pb[b][:, 0:TT], w, rv, start=f, stop=l), deps=dep,
                            noinc=(mi < 15))
                    sl_wdq[wsl].read(mmt)
                    if idx + 3 < len(qseq):
                        wd_load(idx + 3)
                sl_pb[bank].wrote(mmt)
                sl_fuse_tt.read(mmt)
                gsl = j % 2
                ssl_t = j % 2
                tt1 = P.op("scalar", lambda e, b=bank, j=j, g=ssl_t: e.activation(
                    sgm_v[g], pb[b][:, 0:TT], AF.Tanh, scale=sdh_c[:, j:j + 1]),
                    deps=[mmt] + sl_sgm[ssl_t].wdeps() + init_small)
                sl_sgm[ssl_t].wrote(tt1)
                ty2 = P.op("scalar", lambda e, b=bank, g=gsl: e.activation(
                    gfin_v[g], pb[b][:, 0:TT], AF.Identity),
                    deps=[mmt] + sl_gfin[gsl].wdeps())
                sl_pb[bank].read(ty2)
                sl_gfin[gsl].wrote(ty2)
                tg = P.op("vector", lambda e, g=gsl, t=ssl_t: e.scalar_tensor_tensor(
                    gfin_v[g], sgm_v[t], 1.0, gfin_v[g], ALU.add, ALU.mult),
                    deps=[tt1, ty2])
                sl_sgm[ssl_t].read(tg)
                sl_gfin[gsl].wrote(tg)
                ssl = j % 2
                ts = P.op("sync", lambda e, j=j, s=ssl, tt=tt: e.dma_start(
                    out=sfin_v[s], in_=s_d[j, :, tt * TT:(tt + 1) * TT]),
                    deps=sl_sfin[ssl].wdeps() + (first_deps if tt == 0 and j < 2 else []),
                    dma_sem=dma_sfin[ssl])
                sl_sfin[ssl].wrote(ts)
                tx = P.op("sync", lambda e, j=j, s=ssl, tt=tt: e.dma_start(
                    out=xfin_v[s], in_=x_fm[j * 128:(j + 1) * 128, tt * TT:(tt + 1) * TT]),
                    deps=sl_xfin[ssl].wdeps() + (first_deps if tt == 0 and j < 2 else []),
                    dma_sem=dma_xfin[ssl])
                sl_xfin[ssl].wrote(tx)
                tm = P.op("vector", lambda e, s=ssl, g=gsl, j=j: e.scalar_tensor_tensor(
                    sgm_v[s], gfin_v[g], sdh_c[:, j:j + 1], sfin_v[s], ALU.mult, ALU.mult),
                    deps=[ts, tg] + sl_sgm[ssl].wdeps())
                sl_sfin[ssl].read(tm)
                sl_gfin[gsl].read(tm)
                sl_sgm[ssl].wrote(tm)
                ty = P.op("vector", lambda e, s=ssl: e.tensor_tensor(
                    xfin_v[s], sgm_v[s], xfin_v[s], ALU.add),
                    deps=[tm, tx] + sl_xfin[ssl].rds)
                sl_sgm[ssl].read(ty)
                sl_xfin[ssl].wrote(ty)
                st = P.op("gpsimd", lambda e, j=j, s=ssl, tt=tt: e.dma_start(
                    out=y_fm[j * 128:(j + 1) * 128, tt * TT:(tt + 1) * TT], in_=xfin_v[s]),
                    deps=[ty], dma_sem=dma_y[(tt * J + j) % 4])
                sl_xfin[ssl].read(st)
                y_final.append(st)

        ln_unit(0)

        mi = 0
        ln_defer = {}
        deferred_fuse = []
        for ti in range(NTA):
            quota = (M * (ti + 1)) // NTA
            loads, tts = [], []
            while mi < quota:
                def mk(m):
                    st = {}
                    def load():
                        st['tc'] = ffnA_load(m, [])
                    def tt_run(tt, m=m):
                        if deferred_fuse:
                            deferred_fuse.pop(0)()
                        deferred_fuse.append(ffnA_tt(m, tt, st['tc']))
                    return load, [lambda tt=tt: tt_run(tt) for tt in range(NTT)]
                ld_, tt_ = mk(mi)
                if ti == 0 and mi == 0:
                    for t_ln in range(NTT - 1, 0, -1):
                        tt_.insert(t_ln, lambda t=t_ln: ln_unit(t))
                loads.append(ld_)
                tts.append(tt_)
                mi += 1
            micro = []
            if ti in ln_defer and ln_defer[ti] < NTT:
                micro.append(lambda t=ln_defer[ti]: ln_unit(t))
            if loads:
                micro.append(loads[0])
                if len(loads) > 1:
                    micro.append(loads[1])
            for u in range(len(loads)):
                if u + 2 < len(loads):
                    micro.append(loads[u + 2])
                micro.extend(tts[u])

            th = [lambda g=g: att_pre_group(ti, g) for g in range(12)]
            kst = {}
            def _pre():
                kst['k'] = att_pre(ti)
            th.append(_pre)
            th.append(lambda: att_pair_fm(ti, 0, kst['k']))
            if ti > 0:
                th.append(lambda t=ti - 1: att_proj(t))
            for i in range(CH):
                if i + 1 < CH:
                    th.append(lambda i=i: att_pair_fm(ti, i + 1, kst['k']))
                th.append(lambda i=i: att_pair_tree(ti, i))
                th.append(lambda i=i: att_pair_back(ti, i))
            np_ = len(micro)
            nth = len(th)
            for k, t_ in enumerate(th):
                t_()
                want = (np_ * (k + 1)) // nth
                while len(micro) > np_ - want:
                    micro.pop(0)()
            while micro:
                micro.pop(0)()
        while deferred_fuse:
            deferred_fuse.pop(0)()
        att_proj(NTA - 1)

        fuse_done = [(s, s.val) for s in dma_fst]
        s_done = [(s, s.val) for s in dma_sst]
        ln_done = [sl_xln.wr] + sl_xln.rds + sl_lnrow.rds + ([sl_lnrow.wr] if sl_lnrow.wr else [])
        b_first = fuse_done + s_done + ln_done
        for tt in range(NTT):
            ffnB_unit(tt, b_first if tt == 0 else [])

        final_waits = [(s, s.val) for s in dma_y]

        for scc in P.all_sems:
            scc.handle = ctx.enter_context(nc.semaphore(scc.name))
        block = ctx.enter_context(nc.Block())

        @block.sync
        def _(e):
            P.emit_engine(e, "sync")
            for s2, v2 in final_waits:
                e.wait_ge(s2.handle, v2)

        @block.vector
        def _(e):
            P.emit_engine(e, "vector")

        @block.scalar
        def _(e):
            P.emit_engine(e, "scalar")

        @block.tensor
        def _(e):
            P.emit_engine(e, "tensor")

        @block.gpsimd
        def _(e):
            P.emit_engine(e, "gpsimd")

    return nc


def kernel(**inputs):
    x, shared = host_pack(inputs)
    NTOK = x.shape[0] // N_CORES
    nc = build_program(NTOK)
    in_maps = []
    for i in range(N_CORES):
        m = dict(x_fm=np.ascontiguousarray(x[i * NTOK:(i + 1) * NTOK].T))
        m.update(shared)
        in_maps.append(m)
    res = run_bass_kernel_spmd(nc, in_maps, core_ids=list(range(N_CORES)))
    outs = [np.ascontiguousarray(np.asarray(res.results[i]["y_fm"]).T) for i in range(N_CORES)]
    return np.concatenate(outs, axis=0).reshape(4, 4096, D).astype(np.float32)
